# version 15
# speedup vs baseline: 1.0289x; 1.0093x over previous
import contextlib
import numpy as np
import ml_dtypes
import concourse.bass as bass
import concourse.mybir as mybir
from concourse.bass_utils import run_bass_kernel_spmd
from concourse.aot_env import allow_ici_all_to_all

F32 = mybir.dt.float32
BF16 = mybir.dt.bfloat16
AF = mybir.ActivationFunctionType
ALU = mybir.AluOpType
AX = mybir.AxisListType
MAGIC = float(np.float32(1.5 * 2 ** 23))
NEG = -1e30
D = 2048
DT = 16
EPS = 1e-5
bf16 = ml_dtypes.bfloat16

ENGS = ("t", "v", "s", "g", "y")


class DmaSlot:
    def __init__(self, h):
        self.h = h
        self.cnt = 0


class Em:
    def __init__(self, nc):
        self.nc = nc
        self.q = {e: [] for e in ENGS}
        self.esem = {e: nc.alloc_semaphore(name=f"E{e}") for e in ENGS}
        self.ecnt = {e: 0 for e in ENGS}
        self.wm = {e: {} for e in ENGS}
        self.slots = []
        self.cc_sem = nc.alloc_semaphore(name="ccs")
        self.cc_cnt = 0

    def dma_slot(self, name):
        s = DmaSlot(self.nc.alloc_semaphore(name=name))
        self.slots.append(s)
        return s

    def wait(self, e, tok):
        if tok is None:
            return
        sem, val = tok
        if val <= 0:
            return
        k = sem.num if hasattr(sem, "num") else id(sem)
        if self.wm[e].get(k, 0) >= val:
            return
        self.q[e].append(("w", sem, val))
        self.wm[e][k] = val

    def op(self, e, fn, deps=(), inc=True):
        for d in deps:
            self.wait(e, d)
        self.q[e].append(("o", fn, inc, self.esem[e]))
        if inc:
            self.ecnt[e] += 1
            return (self.esem[e], self.ecnt[e])
        return None

    def cc(self, fn, deps=()):
        for d in deps:
            self.wait("g", d)
        self.q["g"].append(("o", fn, True, self.cc_sem))
        self.cc_cnt += 1
        return (self.cc_sem, self.cc_cnt)

    def dma(self, e, slot, fn, deps=()):
        for d in deps:
            self.wait(e, d)
        self.q[e].append(("d", fn, slot.h))
        slot.cnt += 16
        return (slot.h, slot.cnt)

    def dma_percore(self, e, slot, fn, deps=()):
        for d in deps:
            self.wait(e, d)
        h = slot.h

        def wrapped(eng, fn=fn, h=h):
            for arm in eng.switch_core_id(n=128):
                fn(eng, arm.logical % 4).then_inc(h, 16)
            return None
        self.q[e].append(("o", wrapped, False, None))
        slot.cnt += 16
        return (slot.h, slot.cnt)

    def etok(self, e):
        return (self.esem[e], self.ecnt[e])

    def barrier(self):
        toks = [(self.esem[e], self.ecnt[e]) for e in ENGS]
        toks += [(s.h, s.cnt) for s in self.slots]
        toks += [(self.cc_sem, self.cc_cnt)]
        for e in ENGS:
            for t in toks:
                self.wait(e, t)

    def flush(self, name):
        if not any(self.q.values()):
            return
        with self.nc.Block(name) as block:
            for e, dec in (("t", block.tensor), ("v", block.vector),
                           ("s", block.scalar), ("g", block.gpsimd),
                           ("y", block.sync)):
                items = self.q[e]
                if not items:
                    continue

                def body(eng, items=items):
                    for it in items:
                        if it[0] == "w":
                            eng.wait_ge(it[1], it[2])
                        elif it[0] == "o":
                            r = it[1](eng)
                            if it[2]:
                                r.then_inc(it[3], 1)
                        else:
                            it[1](eng).then_inc(it[2], 16)
                dec(body)
        self.q = {e: [] for e in ENGS}


def rearr(ap, p=128):
    return ap.rearrange("(a p) c -> p a c", p=p)


def build(S):
    Sq = S // 4
    NTC = S // 512
    W = min(512, Sq)
    nch = Sq // W
    ntk = Sq // 128
    nc = bass.Bass("TRN2", target_bir_lowering=False)
    em = Em(nc)
    T, V, SS, G, Y = "t", "v", "s", "g", "y"

    def din(name, shape, dt=F32):
        return nc.dram_tensor(name, list(shape), dt, kind="ExternalInput").ap()

    def dscr(name, shape, dt):
        return nc.dram_tensor(name, list(shape), dt).ap()

    u_q = din("u_q", (D, Sq))
    winT = din("winT", (D, 2056))
    convw = din("convw", (128, 12, 4))
    convb = din("convb", (128, 12))
    zb = din("zb", (128, 4))
    drep = din("drep", (128, 4))
    rmsw = din("rmsw", (128, DT))
    woutT = din("woutT", (D, D))
    upw_sl = din("upw_sl", (D, 1024))
    dnw_sl = din("dnw_sl", (8192, 256))
    upsc = din("upsc", (128, 64))
    dnsc = din("dnsc", (128, DT))
    outsc = din("outsc", (1, 1))
    wq_sl = din("wq_sl", (D, 512))
    wk_sl = din("wk_sl", (D, 128))
    wv_sl = din("wv_sl", (D, 128))
    wo = din("wo", (D, D))
    gw_sl = din("gw_sl", (D, 1408))
    dw1_sl = din("dw1_sl", (5632, 256))
    gnw = din("gnw", (128, DT))
    dnw1 = din("dnw1", (128, 44))
    maskT = din("maskT", (128, 128))
    amask = din("amask", (128, 4, 512))
    eye_bf = din("eye_bf", (128, 128), BF16)
    eye_f = din("eye_f", (128, 128))
    sel8 = din("sel8", (8, 1024))

    out_q = nc.dram_tensor("out_q", [D, Sq], F32, kind="ExternalOutput").ap()

    ag0_in = dscr("ag0_in", (D, Sq), BF16)
    ag0_out = dscr("ag0_out", (4 * D, Sq), BF16)
    ag1_in = dscr("ag1_in", (D, Sq), BF16)
    ag1_out = dscr("ag1_out", (4 * D, Sq), BF16)
    a2am_in = dscr("a2am_in", (512, S), BF16)
    a2am_out = dscr("a2am_out", (D, S), BF16)
    a2aa_in = dscr("a2aa_in", (512, S), BF16)
    a2aa_out = dscr("a2aa_out", (D, S), BF16)
    agqg_in = dscr("agqg_in", (D, 1408), BF16)
    agqg_out = dscr("agqg_out", (8 * D, 1408), BF16)
    agqd_in = dscr("agqd_in", (5632, 256), BF16)
    agqd_out = dscr("agqd_out", (8 * 5632, 256), BF16)
    agsu_in = dscr("agsu_in", (D, 1024), BF16)
    agsu_out = dscr("agsu_out", (8 * D, 1024), BF16)
    agsd_in = dscr("agsd_in", (8192, 256), BF16)
    agsd_out = dscr("agsd_out", (8 * 8192, 256), BF16)
    arow = dscr("arow", (128, 2), F32)
    arw_in = dscr("arw_in", (1, 16), F32)
    arw_out = dscr("arw_out", (1, 16), F32)
    h_sp = dscr("h_sp", (5632, Sq), F32)
    xq2_d = dscr("xq2_d", (5632, Sq), BF16)

    G_BATCH = [[0, 1, 2, 3], [4, 5, 6, 7]]
    G_ALL = [[0, 1, 2, 3, 4, 5, 6, 7]]

    so_c = em.dma_slot("so_c")
    su = [em.dma_slot(f"su{i}") for i in range(4)]
    sstp = [em.dma_slot(f"sstp{i}") for i in range(2)]
    sw = [em.dma_slot(f"sw{i}") for i in range(2)]
    sqs = [em.dma_slot(f"sqs{i}") for i in range(2)]
    smisc = em.dma_slot("smisc")
    sgm = em.dma_slot("sgm")
    sg2 = [em.dma_slot(f"sg2{i}") for i in range(2)]
    sr2 = [em.dma_slot(f"sr2{i}") for i in range(2)]
    syg = [em.dma_slot(f"syg{i}") for i in range(2)]
    swu = [em.dma_slot(f"swu{i}") for i in range(2)]
    swd = [em.dma_slot(f"swd{i}") for i in range(2)]
    sqg = [em.dma_slot(f"sqg{i}") for i in range(4)]
    sh2 = [em.dma_slot(f"sh2{i}") for i in range(2)]
    shl = [em.dma_slot(f"shl{i}") for i in range(2)]
    sq2s = [em.dma_slot(f"sq2s{i}") for i in range(2)]
    srh = [em.dma_slot(f"srh{i}") for i in range(2)]

    def coll(kind, op, groups, i_ap, o_ap, deps=()):
        def fn(g):
            with allow_ici_all_to_all():
                return g.collective_compute(
                    kind, op, replica_groups=groups, ins=[i_ap.opt()],
                    outs=[o_ap.opt()])
        return em.cc(fn, deps=deps)

    root = contextlib.ExitStack()

    def sbr(name, shape, dt=F32):
        return root.enter_context(nc.sbuf_tensor(name, list(shape), dt))

    ones1f = sbr("ones1f", (128, 128))
    onesLN = sbr("onesLN", (128, 128))
    ones56 = sbr("ones56", (128, 128))
    ones1b = sbr("ones1b", (128, 128), BF16)
    eyeb_sb = sbr("eyeb", (128, 128), BF16)
    eyef_sb = sbr("eyef", (128, 128))
    maskT_sb = sbr("maskTs", (128, 128))
    convw_sb = sbr("convws", (128, 12, 4))
    convb_sb = sbr("convbs", (128, 12))
    zb_sb = sbr("zbs", (128, 4))
    drep_sb = sbr("dreps", (128, 4))
    rmsw_sb = sbr("rmsws", (128, DT))
    upsc_sb = sbr("upscs", (128, 64))
    dnsc_sb = sbr("dnscs", (128, DT))
    outsc_sb = sbr("outscs", (1, 1))
    dscmb_sb = sbr("dscmb", (128, DT))
    gnw_sb = sbr("gnws", (128, DT))
    dnw1_sb = sbr("dnw1s", (128, 44))
    srep_sb = sbr("sreps", (128, 2))
    invs_sb = sbr("invss", (128, 2))
    row2 = sbr("row2", (1, 2))
    row16 = sbr("row16", (1, 16))
    rep1 = sbr("rep1", (128, 1))
    cst_sb = sbr("cst_sb", (128, 4))
    late = contextlib.ExitStack()
    sel8_sb = sbr("sel8s", (8, 1024))

    c0 = em.op(G, lambda g: g.memset(ones1f[:, :], 1.0))
    c1 = em.op(G, lambda g: g.memset(onesLN[:, :], 1.0 / 2048.0))
    c2 = em.op(G, lambda g: g.memset(ones56[:, :], 1.0 / 5632.0))
    c3 = em.op(G, lambda g: g.memset(ones1b[:, :], 1.0))
    em.op(G, lambda g: g.memset(cst_sb[:, 0:1], EPS))
    em.op(G, lambda g: g.memset(cst_sb[:, 1:2], -8.0))
    em.op(G, lambda g: g.memset(cst_sb[:, 2:3], -0.5))
    c4 = em.op(G, lambda g: g.memset(cst_sb[:, 3:4], 0.5))
    dldr = []
    for dst, src in ((eyeb_sb, eye_bf), (eyef_sb, eye_f), (maskT_sb, maskT),
                     (convw_sb, convw), (convb_sb, convb), (zb_sb, zb),
                     (drep_sb, drep), (rmsw_sb, rmsw), (upsc_sb, upsc),
                     (dnsc_sb, dnsc), (outsc_sb, outsc), (gnw_sb, gnw),
                     (dnw1_sb, dnw1),
                     (sel8_sb, sel8)):
        nsl = len(dst.shape)
        dldr.append(em.dma(Y, so_c, lambda y, a=dst, b=src, n=nsl: y.dma_start(
            out=a[(slice(None),) * n], in_=b[(slice(None),) * n])))
    tok_c = dldr[-1]
    tok_eyeb = tok_eyef = tok_c

    es_win = contextlib.ExitStack()
    win_bf = es_win.enter_context(
        nc.sbuf_tensor("win_bf", [128, DT, 2056], BF16))
    ldw = em.dma(G, sgm, lambda g: g.dma_start(
        out=win_bf[:, :, :], in_=rearr(winT)[:, :, :]))
    es0 = contextlib.ExitStack()

    def sb0(name, shape, dt=F32):
        return es0.enter_context(nc.sbuf_tensor(name, list(shape), dt))
    xst = sb0("xst", (128, 4, Sq))
    sqst = sb0("sqst", (128, 2, Sq))
    m_sb = sb0("m_sb", (128, Sq))
    rs_sb = sb0("rs_sb", (128, Sq))
    v131 = sb0("v131", (128, Sq))
    xnst = sb0("xnst", (128, 2, Sq), BF16)
    ps_m = es0.enter_context(nc.psum_tensor("ps_m", [128, Sq], F32))
    ps_v = es0.enter_context(nc.psum_tensor("ps_v", [128, Sq], F32))
    ps_r = es0.enter_context(nc.psum_tensor("ps_r", [128, 128], F32))

    ld = [None] * DT
    sq = [None] * DT
    mmv = [None] * DT
    for dt in range(DT):
        r4 = dt % 4
        ld[dt] = em.dma(Y, su[r4], lambda y, dt=dt, r4=r4: y.dma_start(
            out=xst[:, r4, :], in_=rearr(u_q)[:, dt, :]),
            deps=[mmv[dt - 4]] if dt >= 4 else [])
        sq[dt] = em.op(SS, lambda s, dt=dt, r4=r4: s.activation(
            sqst[:, dt % 2, :], xst[:, r4, :], AF.Square),
            deps=[ld[dt]] + ([mmv[dt - 2]] if dt >= 2 else []))
        for ch in range(nch):
            em.op(T, lambda t, r4=r4, ch=ch, dt=dt: t.matmul(
                ps_m[:, ch * W:ch * W + W], onesLN[:, :],
                xst[:, r4, ch * W:ch * W + W],
                start=(dt == 0), stop=(dt == DT - 1)),
                deps=([ld[dt], c1] if ch == 0 else []), inc=False)
            tk = em.op(T, lambda t, ch=ch, dt=dt: t.matmul(
                ps_v[:, ch * W:ch * W + W], onesLN[:, :],
                sqst[:, dt % 2, ch * W:ch * W + W],
                start=(dt == 0), stop=(dt == DT - 1)),
                deps=([sq[dt]] if ch == 0 else []), inc=(ch == nch - 1))
        mmv[dt] = tk
    tstat = mmv[DT - 1]
    t5 = em.op(V, lambda v: v.tensor_copy(m_sb[:, :], ps_m[:, :]), deps=[tstat])
    t1 = em.op(V, lambda v: v.tensor_tensor(v131[:, :], m_sb[:, :], m_sb[:, :],
                                            ALU.mult), deps=[t5])
    t2 = em.op(V, lambda v: v.tensor_tensor(v131[:, :], ps_v[:, :], v131[:, :],
                                            ALU.subtract), deps=[t1])
    t3 = em.op(SS, lambda s: s.activation(v131[:, :], v131[:, :], AF.Sqrt,
                                          bias=cst_sb[:, 0:1]), deps=[t2, c4])
    t4 = em.op(V, lambda v: v.reciprocal(rs_sb[:, :], v131[:, :]), deps=[t3])
    std = [None] * DT
    apl = [None] * DT
    for dt in range(DT):
        r4 = dt % 4
        l2 = em.dma(Y, su[r4], lambda y, dt=dt, r4=r4: y.dma_start(
            out=xst[:, r4, :], in_=rearr(u_q)[:, dt, :]),
            deps=[t5, sq[dt]] + ([std[dt - 4]] if dt >= 4 else []))
        a1 = em.op(V, lambda v, r4=r4: v.tensor_tensor(
            v131[:, :], xst[:, r4, :], m_sb[:, :], ALU.subtract),
            deps=[l2, t4] + ([apl[dt - 1]] if dt >= 1 else []))
        a2 = em.op(V, lambda v, dt=dt: v.tensor_tensor(
            xnst[:, dt % 2, :], v131[:, :], rs_sb[:, :], ALU.mult),
            deps=[a1] + ([std[dt - 2]] if dt >= 2 else []))
        apl[dt] = a2
        std[dt] = em.dma(Y, sstp[dt % 2], lambda y, dt=dt: y.dma_start(
            out=rearr(ag0_in)[:, dt, :], in_=xnst[:, dt % 2, :]), deps=[a2])
    tok_ag0 = coll("AllGather", ALU.bypass, G_BATCH, ag0_in, ag0_out,
                   deps=[std[DT - 1], std[DT - 2]])

    k2 = em.dma(Y, smisc, lambda y: y.dma_start(
        out=rep1[:, :], in_=outsc[0:1, 0:1].broadcast_to([128, 1])))
    k3 = em.op(V, lambda v: v.tensor_scalar(dscmb_sb[:, :], dnsc_sb[:, :],
                                            rep1[:, 0:1], None, ALU.mult),
               deps=[k2, tok_c])
    toks_b = [(em.esem[e], em.ecnt[e]) for e in ENGS]
    toks_b += [(s.h, s.cnt) for s in em.slots if s is not sgm]
    for e_ in ENGS:
        for t_ in toks_b:
            em.wait(e_, t_)
    em.flush("p0")
    es0.close()

    es1 = contextlib.ExitStack()

    def sb1(name, shape, dt=F32):
        return es1.enter_context(nc.sbuf_tensor(name, list(shape), dt))
    rhs_rg = sb1("rhs_rg", (128, DT, 512), BF16)
    stage_x = sb1("stage_x", (128, 12, 516))
    cacc = sb1("cacc", (128, 2, 512))
    xbc_bf = sb1("xbc_bf", (128, 2, 12, 512), BF16)
    zg_ch = sb1("zg_ch", (128, 4, 512))
    alog = sb1("alog", (8, 512))
    last_scn = {}
    a_tmp = sb1("a_tmp", (8, 512))
    cs_sb = sb1("cs_sb", (8, 128))
    cstm = sb1("cstm", (128, 8))
    dectm = sb1("dectm", (128, 8))
    d0_sb = sb1("d0_sb", (8, 128))
    ecs = sb1("ecs", (8, 128))
    dcrow = sb1("dcrow", (1, 8))
    decpair = sb1("decpair", (128, 4))
    arg_sb = sb1("arg_sb", (128, 1024))
    pexp = sb1("pexp", (128, 1024), BF16)
    prod = sb1("prod", (128, 1024), BF16)
    xtm = sb1("xtm", (128, 4, 128), BF16)
    btm = sb1("btm", (128, 4, 128), BF16)
    xdtm = sb1("xdtm", (128, 4, 128), BF16)
    cexp = sb1("cexp", (128, 4, 128), BF16)
    state_f = sb1("state_f", (128, 4, 64))
    state_bfz = sb1("state_bfz", (128, 8, 64), BF16)
    bm_e = sb1("bm_e", (128, 4, 128), BF16)
    bm_o = sb1("bm_o", (128, 4, 128), BF16)
    ygst = sb1("ygst", (128, 2, 4, 128), BF16)
    ps_ip = es1.enter_context(nc.psum_tensor("ps_ip", [128, 2, 512], F32))
    ps_g = es1.enter_context(nc.psum_tensor("ps_g", [128, 1024], F32))
    ps_y = es1.enter_context(nc.psum_tensor("ps_y", [128, 512], F32))
    ps_st = es1.enter_context(nc.psum_tensor("ps_st", [128, 256], F32))
    ps_e = es1.enter_context(nc.psum_tensor("ps_e", [128, 512], F32))
    ps_tr = es1.enter_context(nc.psum_tensor("ps_tr", [128, 512], BF16))
    qres = {}
    qsteps = []

    def _mk(fn):
        qsteps.append(fn)

    sgq = em.dma_slot("sgq")
    wsl = sb1("wsl", (128, 2, 1408))
    acc_sb = sb1("acc_sb", (128, 64))
    qa = sb1("qa", (128, 2, 1408), BF16)
    qb = sb1("qb", (128, 1408), BF16)
    red2f = sb1("red2f", (1, 260))
    invsr = sb1("invsr", (128, 2))
    qst = {"ab": [None] * 60, "qtok": [None] * 16, "qtok2": [None] * 44,
           "stok": [None] * 16, "stok2": [None] * 64, "v1p": None}

    for i in range(16):
        def step(i=i):
            ab = qst["ab"]
            l = em.dma(G, sw[i % 2], lambda y, i=i: y.dma_start(
                out=wsl[:, i % 2, :], in_=rearr(gw_sl)[:, i, :]),
                deps=[ab[i - 2]] if i >= 2 else [])
            ab[i] = em.op(SS, lambda s, i=i: s.activation(
                wsl[:, i % 2, :], wsl[:, i % 2, :], AF.Abs,
                accum_out=acc_sb[:, i:i + 1]), deps=[l])
        _mk(step)
    for i in range(44):
        def step(i=i):
            ab = qst["ab"]
            l = em.dma(G, sw[i % 2], lambda y, i=i: y.dma_start(
                out=wsl[:, i % 2, 0:256], in_=rearr(dw1_sl)[:, i, :]),
                deps=[ab[16 + i - 2]])
            ab[16 + i] = em.op(SS, lambda s, i=i: s.activation(
                wsl[:, i % 2, 0:256], wsl[:, i % 2, 0:256], AF.Abs,
                accum_out=acc_sb[:, 16 + i:17 + i]), deps=[l])
        _mk(step)

    def step_ar():
        ab = qst["ab"]
        r1 = em.op(V, lambda v: v.reduce_sum(acc_sb[:, 60:61], acc_sb[:, 0:16],
                                             AX.X), deps=[ab[15]])
        r2 = em.op(V, lambda v: v.reduce_sum(acc_sb[:, 61:62], acc_sb[:, 16:60],
                                             AX.X), deps=[ab[59]])
        d1 = em.dma(G, sgq, lambda y: y.dma_start(
            out=arow[:, :], in_=acc_sb[:, 60:62]), deps=[r1, r2])
        d2 = em.dma(G, sgq, lambda y: y.dma_start(
            out=red2f[0:1, 4:260],
            in_=arow[:, :].rearrange("a b -> (a b)")[None, :]), deps=[d1])
        r5 = em.op(V, lambda v: v.reduce_sum(
            red2f[0:1, 0:1],
            red2f[0:1, 4:260].rearrange("a (b c) -> a b c", c=2)[:, :, 0],
            AX.XY), deps=[d2])
        r6 = em.op(V, lambda v: v.reduce_sum(
            red2f[0:1, 1:2],
            red2f[0:1, 4:260].rearrange("a (b c) -> a b c", c=2)[:, :, 1],
            AX.XY), deps=[d2, r5])
        z16 = em.op(V, lambda v: v.memset(row16[:, :], 0.0))
        zd16 = em.dma(G, sgq, lambda y: y.dma_start(
            out=arw_in[0:1, :], in_=row16[:, :]), deps=[z16])
        star = em.dma(G, sgq, lambda y: y.dma_start(
            out=arw_in[0:1, 0:2], in_=red2f[0:1, 0:2]), deps=[r6, zd16])
        tok_ar = coll("AllReduce", ALU.add, G_ALL, arw_in, arw_out,
                      deps=[star])
        ldar = em.dma(G, sgq, lambda y: y.dma_start(
            out=invsr[:, :], in_=arw_out[0:1, 0:2].broadcast_to([128, 2])),
            deps=[tok_ar])
        q2a = em.op(V, lambda v: v.tensor_scalar(
            invs_sb[:, 0:1], invsr[:, 0:1], 1.0 / (2048.0 * 11264.0), 1e-5,
            ALU.mult, ALU.max), deps=[ldar])
        q2 = em.op(V, lambda v: v.tensor_scalar(
            invs_sb[:, 1:2], invsr[:, 1:2], 1.0 / (2048.0 * 5632.0), 1e-5,
            ALU.mult, ALU.max), deps=[q2a])
        q3 = em.op(V, lambda v: v.reciprocal(srep_sb[:, :], invs_sb[:, :]),
                   deps=[q2])
        qres["q2"] = q2
        qres["q3"] = q3
    _mk(step_ar)

    for i in range(16):
        def step(i=i):
            qtok, ab = qst["qtok"], qst["ab"]
            l = em.dma(G, sw[i % 2], lambda y, i=i: y.dma_start(
                out=wsl[:, i % 2, :], in_=rearr(gw_sl)[:, i, :]),
                deps=[ab[59], qtok[i - 2] if i >= 2 else None])
            s1 = em.op(SS, lambda s, i=i: s.activation(
                qa[:, i % 2, :], wsl[:, i % 2, :], AF.Sign, bias=cst_sb[:, 2:3],
                scale=srep_sb[:, 0:1]),
                deps=[l, qres["q3"], c4, qtok[i - 2] if i >= 2 else None])
            s2 = em.op(SS, lambda s, i=i: s.activation(
                qb[:, :], wsl[:, i % 2, :], AF.Sign, bias=cst_sb[:, 3:4],
                scale=srep_sb[:, 0:1]), deps=[s1, qst["v1p"]])
            v1 = em.op(V, lambda v, i=i: v.tensor_tensor(
                qa[:, i % 2, :], qa[:, i % 2, :], qb[:, :], ALU.add),
                deps=[s2])
            qst["v1p"] = v1
            qtok[i] = em.dma(G, sqs[i % 2], lambda y, i=i: y.dma_start(
                out=rearr(agqg_in)[:, i, :], in_=qa[:, i % 2, :]), deps=[v1])
            if i == 15:
                qres["agqg"] = coll("AllGather", ALU.bypass, G_ALL, agqg_in,
                                    agqg_out,
                                    deps=[qtok[-1], qtok[-2], tok_ag0])
        _mk(step)
    for i in range(44):
        def step(i=i):
            qtok, qtok2 = qst["qtok"], qst["qtok2"]
            l = em.dma(G, sw[i % 2], lambda y, i=i: y.dma_start(
                out=wsl[:, i % 2, 0:256], in_=rearr(dw1_sl)[:, i, :]),
                deps=[qtok2[i - 2]] if i >= 2 else [qtok[-1], qtok[-2]])
            s1 = em.op(SS, lambda s, i=i: s.activation(
                qa[:, i % 2, 0:256], wsl[:, i % 2, 0:256], AF.Sign,
                bias=cst_sb[:, 2:3], scale=srep_sb[:, 1:2]),
                deps=[l, qtok2[i - 2] if i >= 2 else qtok[i - 2 + 16]])
            s2 = em.op(SS, lambda s, i=i: s.activation(
                qb[:, 0:256], wsl[:, i % 2, 0:256], AF.Sign,
                bias=cst_sb[:, 3:4], scale=srep_sb[:, 1:2]),
                deps=[s1, qst["v1p"]])
            v1 = em.op(V, lambda v, i=i: v.tensor_tensor(
                qa[:, i % 2, 0:256], qa[:, i % 2, 0:256], qb[:, 0:256],
                ALU.add), deps=[s2])
            qst["v1p"] = v1
            qtok2[i] = em.dma(G, sqs[i % 2], lambda y, i=i: y.dma_start(
                out=rearr(agqd_in)[:, i, :], in_=qa[:, i % 2, 0:256]),
                deps=[v1])
            if i == 43:
                qres["agqd"] = coll("AllGather", ALU.bypass, G_ALL, agqd_in,
                                    agqd_out, deps=[qtok2[-1], qtok2[-2]])
        _mk(step)
    for i in range(16):
        def step(i=i):
            qtok2, stok = qst["qtok2"], qst["stok"]
            l = em.dma(G, sw[i % 2], lambda y, i=i: y.dma_start(
                out=wsl[:, i % 2, 0:1024], in_=rearr(upw_sl)[:, i, :]),
                deps=[stok[i - 2]] if i >= 2 else [qtok2[-1], qtok2[-2]])
            s1 = em.op(SS, lambda s, i=i: s.activation(
                qa[:, i % 2, 0:1024], wsl[:, i % 2, 0:1024], AF.Sign),
                deps=[l, stok[i - 2] if i >= 2 else qtok2[i - 2 + 44],
                      qst["v1p"]])
            stok[i] = em.dma(G, sqs[i % 2], lambda y, i=i: y.dma_start(
                out=rearr(agsu_in)[:, i, :], in_=qa[:, i % 2, 0:1024]),
                deps=[s1])
            if i == 15:
                qres["agsu"] = coll("AllGather", ALU.bypass, G_ALL, agsu_in,
                                    agsu_out, deps=[stok[-1], stok[-2]])
        _mk(step)
    for i in range(64):
        def step(i=i):
            stok, stok2 = qst["stok"], qst["stok2"]
            l = em.dma(G, sw[i % 2], lambda y, i=i: y.dma_start(
                out=wsl[:, i % 2, 0:256], in_=rearr(dnw_sl)[:, i, :]),
                deps=[stok2[i - 2]] if i >= 2 else [stok[-1], stok[-2]])
            s1 = em.op(SS, lambda s, i=i: s.activation(
                qa[:, i % 2, 0:256], wsl[:, i % 2, 0:256], AF.Sign),
                deps=[l, stok2[i - 2] if i >= 2 else stok[i - 2 + 16]])
            stok2[i] = em.dma(G, sqs[i % 2], lambda y, i=i: y.dma_start(
                out=rearr(agsd_in)[:, i, :], in_=qa[:, i % 2, 0:256]),
                deps=[s1])
            if i == 63:
                qres["agsd"] = coll("AllGather", ALU.bypass, G_ALL, agsd_in,
                                    agsd_out, deps=[stok2[-1], stok2[-2]])
        _mk(step)

    z0 = em.op(V, lambda v: v.memset(stage_x[:, :, 0:3], 0.0))
    z1 = em.op(V, lambda v: v.memset(state_f[:, :, :], 0.0))
    z2 = em.op(V, lambda v: v.memset(state_bfz[:, :, :], 0.0))
    z3 = em.op(V, lambda v: v.memset(bm_e[:, :, :], 0.0))
    z4 = em.op(V, lambda v: v.memset(bm_o[:, :, :], 0.0))

    spin_tok = None
    for i_ in range(600):
        spin_tok = em.op(T, lambda t: t.matmul(
            ps_ip[:, 0, 0:128], eyef_sb[:, :], eyef_sb[:, :],
            start=True, stop=True), inc=(i_ == 599))

    evac = {}
    conv_v4 = {}
    conv_tok = {}
    zg_tok = {}
    alog_tok = {}
    yg_dma = []
    tc_done = {}
    pc = {"su": z1, "bd": z2, "ev": None, "kk": None, "ke": None, "tr2": None,
          "dp": None, "pr1": None, "xd": None, "cxp": None, "g2": None,
          "cvx": None, "cvb": None, "e2s": [z0], "scn": None, "g3": None,
          "cv4": None, "ymm0": None, "gmm": None}
    carry_tok = z0

    for tc in range(NTC):
        rr = tc % 2
        r_blk = (512 * tc) // Sq
        c0_ = (512 * tc) % Sq
        rtok = em.dma(Y, sr2[rr], lambda y, r_blk=r_blk, c0_=c0_:
                      y.dma_start(out=rhs_rg[:, :, :],
                                  in_=rearr(ag0_out)[:, 16 * r_blk:16 * r_blk + 16,
                                                     c0_:c0_ + 512]),
                      deps=[tok_ag0, evac.get((tc - 1, 16))] +
                      list(tc_done.get(tc - 2, [])))
        if tc > 0:
            carry_tok = em.op(V, lambda v: v.tensor_copy(
                stage_x[:, :, 0:3], stage_x[:, :, 512:515]),
                deps=[conv_v4[(tc - 1, 11)]])
        for oc in range(17):
            for _ in range(min(2, len(qsteps))):
                qsteps.pop(0)()
            pr = oc % 2
            wid = 128 if oc < 16 else 8
            co = oc * 128
            deps0 = [rtok, ldw, evac.get((tc, oc - 2)),
                     evac.get((tc - 1, 16 - oc)) if oc < 2 else None,
                     spin_tok if (tc == 0 and oc == 0) else None]
            for dt in range(DT):
                tkmm = em.op(T, lambda t, dt=dt, pr=pr, co=co, wid=wid:
                             t.matmul(ps_ip[0:wid, pr, :],
                                      win_bf[:, dt, co:co + wid],
                                      rhs_rg[:, dt, :],
                                      start=(dt == 0), stop=(dt == DT - 1)),
                             deps=deps0 if dt == 0 else [], inc=(dt == DT - 1))
            if oc < 12:
                ev = em.op(SS, lambda s, oc=oc, pr=pr: s.copy(
                    stage_x[:, oc, 3:515], ps_ip[:, pr, :]),
                    deps=[tkmm, carry_tok, conv_v4.get((tc - 1, oc))])
            elif oc < 16:
                zi = oc - 12
                ev = em.op(SS, lambda s, zi=zi, pr=pr: s.activation(
                    zg_ch[:, zi, :], ps_ip[:, pr, :], AF.Silu,
                    bias=zb_sb[:, zi:zi + 1]),
                    deps=[tkmm] + list(tc_done.get(tc - 1, [])))
                zg_tok[(tc, zi)] = ev
            else:
                e1 = em.op(SS, lambda s, pr=pr: s.activation(
                    a_tmp[:, :], ps_ip[0:8, pr, :], AF.Exp),
                    deps=[tkmm, alog_tok.get(tc - 1), last_scn.get(tc - 1)])
                e2 = em.op(V, lambda v: v.tensor_scalar(
                    a_tmp[:, :], a_tmp[:, :], 1.0, None, ALU.add), deps=[e1])
                ev = em.op(SS, lambda s: s.activation(
                    alog[:, :], a_tmp[:, :], AF.Ln), deps=[e2])
                alog_tok[tc] = ev
            evac[(tc, oc)] = ev
        for j in range(12):
            cj = j % 2
            v1 = em.op(V, lambda v, j=j, cj=cj: v.tensor_scalar(
                cacc[:, cj, :], stage_x[:, j, 0:512], convw_sb[:, j, 0:1],
                None, ALU.mult),
                deps=[evac[(tc, j)], carry_tok, conv_tok.get((tc, j - 2)),
                      conv_tok.get((tc - 1, j + 10)) if j < 2 else None,
                      tok_c])
            v2 = em.op(V, lambda v, j=j, cj=cj: v.scalar_tensor_tensor(
                cacc[:, cj, :], stage_x[:, j, 1:513], convw_sb[:, j, 1:2],
                cacc[:, cj, :], ALU.mult, ALU.add), deps=[v1])
            v3 = em.op(V, lambda v, j=j, cj=cj: v.scalar_tensor_tensor(
                cacc[:, cj, :], stage_x[:, j, 2:514], convw_sb[:, j, 2:3],
                cacc[:, cj, :], ALU.mult, ALU.add), deps=[v2])
            v4 = em.op(V, lambda v, j=j, cj=cj: v.scalar_tensor_tensor(
                cacc[:, cj, :], stage_x[:, j, 3:515], convw_sb[:, j, 3:4],
                cacc[:, cj, :], ALU.mult, ALU.add), deps=[v3])
            conv_v4[(tc, j)] = v4
            conv_tok[(tc, j)] = em.op(SS, lambda s, j=j, cj=cj, rr=rr:
                                      s.activation(xbc_bf[:, rr, j, :],
                                                   cacc[:, cj, :], AF.Silu,
                                                   bias=convb_sb[:, j:j + 1]),
                                      deps=[v4, conv_tok.get((tc - 1, j))] +
                                      list(tc_done.get(tc - 2, [])))
        for ci in range(4):
            cL = 128 * ci
            scn = em.op(V, lambda v, cL=cL: v.tensor_tensor_scan(
                cs_sb[:, :], alog[:, cL:cL + 128], alog[:, cL:cL + 128],
                0.0, ALU.add, ALU.bypass),
                deps=[alog_tok[tc], pc["kk"], pc["ke"], pc["tr2"], pc["g2"]])
            last_scn[tc] = scn
            tr1 = em.op(T, lambda t: t.transpose(ps_e[:, 0:8], cs_sb[:, :],
                                                 eyef_sb[0:8, 0:8]),
                        deps=[scn, pc["cv4"], tok_eyef])
            cv1 = em.op(V, lambda v: v.tensor_copy(cstm[:, :], ps_e[:, 0:8]),
                        deps=[tr1, pc["g2"]])
            dv = em.op(V, lambda v: v.tensor_scalar(
                d0_sb[:, :], cs_sb[:, :], cs_sb[:, 127:128], None, ALU.subtract),
                deps=[scn, pc["tr2"]])
            de = em.op(SS, lambda s: s.activation(d0_sb[:, :], d0_sb[:, :],
                                                  AF.Exp), deps=[dv])
            tr2 = em.op(T, lambda t: t.transpose(ps_e[:, 8:16], d0_sb[:, :],
                                                 eyef_sb[0:8, 0:8]),
                        deps=[de, cv1])
            cv2 = em.op(V, lambda v: v.tensor_copy(dectm[:, :], ps_e[:, 8:16]),
                        deps=[tr2, pc["xd"]])
            pc["tr2"] = tr2
            ee = em.op(SS, lambda s: s.activation(ecs[:, :], cs_sb[:, :], AF.Exp,
                                                  scale=-1.0),
                       deps=[scn, pc["ke"]])
            tr3 = em.op(T, lambda t: t.transpose(ps_e[0:1, 16:24],
                                                 ecs[:, 127:128],
                                                 eyef_sb[0:8, 0:8]),
                        deps=[ee, cv2])
            cv3 = em.op(V, lambda v: v.tensor_copy(dcrow[0:1, :],
                                                   ps_e[0:1, 16:24]),
                        deps=[tr3, pc["dp"]])
            dpl = None
            for p in range(4):
                em.op(T, lambda t, p=p: t.matmul(
                    ps_e[0:64, 24 + p:25 + p], ones1f[0:1, 0:64],
                    dcrow[0:1, 2 * p:2 * p + 1], start=True, stop=True),
                    deps=[cv3] if p == 0 else [], inc=False)
                dpl = em.op(T, lambda t, p=p: t.matmul(
                    ps_e[64:128, 24 + p:25 + p], ones1f[0:1, 0:64],
                    dcrow[0:1, 2 * p + 1:2 * p + 2], start=True, stop=True,
                    tile_position=(0, 64)), inc=(p == 3))
            pc["dp"] = dpl
            cv4 = em.op(V, lambda v: v.tensor_copy(decpair[:, :],
                                                   ps_e[:, 24:28]),
                        deps=[dpl, pc["su"]])
            pc["cv4"] = cv4
            kk = None
            for h in range(8):
                kk = em.op(T, lambda t, h=h: t.matmul(
                    ps_g[:, 128 * h:128 * h + 128],
                    sel8_sb[:, 128 * h:128 * h + 128],
                    cs_sb[:, :], start=True, stop=True),
                    deps=[scn, pc["pr1"], tok_c] if h == 0 else [],
                    inc=(h == 7))
            pc["kk"] = kk
            g1 = em.op(V, lambda v: v.tensor_tensor(
                arg_sb[:, :].rearrange("p (h l) -> p h l", h=8),
                cstm[:, :, None].broadcast_to([128, 8, 128]),
                ps_g[:, :].rearrange("p (h l) -> p h l", h=8), ALU.subtract),
                deps=[kk, cv1] + pc["e2s"])
            g2 = em.op(V, lambda v: v.tensor_tensor(
                arg_sb[:, :].rearrange("p (h l) -> p h l", h=8),
                arg_sb[:, :].rearrange("p (h l) -> p h l", h=8),
                maskT_sb[:, None, :].broadcast_to([128, 8, 128]), ALU.min),
                deps=[g1, dldr[2]])
            pc["g2"] = g2
            g3 = em.op(SS, lambda s: s.activation(pexp[:, :], arg_sb[:, :],
                                                  AF.Exp),
                       deps=[g2, pc["pr1"]])
            pc["g3"] = g3
            bcp = None
            for p in range(4):
                b1_ = em.op(V, lambda v, p=p, rr=rr, cL=cL: v.tensor_copy(
                    bm_e[0:64, p, :], xbc_bf[0:64, rr, 4 + p, cL:cL + 128]),
                    deps=[conv_tok[(tc, 4 + p)], z3, z4, pc["gmm"]]
                    if p == 0 else [conv_tok[(tc, 4 + p)]])
                bcp = em.op(V, lambda v, p=p, rr=rr, cL=cL: v.tensor_copy(
                    bm_o[64:128, p, :], xbc_bf[64:128, rr, 4 + p, cL:cL + 128]),
                    deps=[b1_])
            gmm = None
            for h in range(8):
                p = h // 2
                bm = bm_e if h % 2 == 0 else bm_o
                gmm = em.op(T, lambda t, h=h, p=p, bm=bm, rr=rr, cL=cL:
                            t.matmul(ps_g[:, 128 * h:128 * h + 128],
                                     bm[:, p, :],
                                     xbc_bf[:, rr, 8 + p, cL:cL + 128],
                                     start=True, stop=True),
                            deps=([g1, bcp] + [conv_tok[(tc, jj)] for jj in
                                               range(8, 12)]) if h == 0 else [],
                            inc=(h == 7))
            pc["gmm"] = gmm
            pr1 = em.op(V, lambda v: v.tensor_tensor(
                prod[:, :], pexp[:, :], ps_g[:, :], ALU.mult),
                deps=[g3, gmm, pc["ymm0"]])
            pc["pr1"] = pr1
            trxl = None
            for j in range(4):
                trxl = em.op(T, lambda t, j=j, rr=rr, cL=cL: t.transpose(
                    ps_tr[:, 128 * j:128 * j + 128],
                    xbc_bf[:, rr, j, cL:cL + 128], eyeb_sb[:, :]),
                    deps=[conv_tok[(tc, j)], tok_eyeb, pc["cvx"]]
                    if j == 0 else [], inc=(j == 3))
            cvx = em.op(V, lambda v: v.tensor_copy(
                xtm[:, :, :], ps_tr[:, :].rearrange("p (a l) -> p a l", a=4)),
                deps=[trxl, pc["xd"], pc["ymm0"]])
            trbl = None
            for j in range(4):
                trbl = em.op(T, lambda t, j=j, rr=rr, cL=cL: t.transpose(
                    ps_tr[:, 128 * j:128 * j + 128],
                    xbc_bf[:, rr, 4 + j, cL:cL + 128], eyeb_sb[:, :]),
                    deps=[conv_tok[(tc, 4 + j)], cvx] if j == 0 else [],
                    inc=(j == 3))
            cvb = em.op(V, lambda v: v.tensor_copy(
                btm[:, :, :], ps_tr[:, :].rearrange("p (a l) -> p a l", a=4)),
                deps=[trbl])
            pc["cvb"] = cvb
            xd = em.op(V, lambda v: v.tensor_tensor(
                xdtm[:, :, :].rearrange("p a (h x) -> p (a h) x", h=2),
                xtm[:, :, :].rearrange("p a (h x) -> p (a h) x", h=2),
                dectm[:, :, None].broadcast_to([128, 8, 64]), ALU.mult),
                deps=[cvx, cv2])
            pc["xd"] = xd
            pc["cvx"] = cvx
            ke = None
            for p in range(4):
                em.op(T, lambda t, p=p: t.matmul(
                    ps_e[0:64, 128 * p:128 * p + 128],
                    sel8_sb[:, 256 * p:256 * p + 64],
                    ecs[:, :], start=True, stop=True),
                    deps=[ee, cv4, g2, cv1, cv2, cv3] if p == 0 else [],
                    inc=False)
                ke = em.op(T, lambda t, p=p: t.matmul(
                    ps_e[64:128, 128 * p:128 * p + 128],
                    sel8_sb[:, 256 * p + 128:256 * p + 192],
                    ecs[:, :], start=True, stop=True,
                    tile_position=(0, 64)), inc=(p == 3))
            pc["ke"] = ke
            cxp = em.op(V, lambda v, rr=rr, cL=cL: v.tensor_tensor(
                cexp[:, :, :],
                xbc_bf[:, rr, 8:12, cL:cL + 128],
                ps_e[:, :].rearrange("p (a l) -> p a l", a=4), ALU.mult),
                deps=[ke] + [conv_tok[(tc, jj)] for jj in range(8, 12)])
            pc["cxp"] = cxp
            smml = None
            first = True
            for p in range(4):
                for par in (0, 1):
                    h = 2 * p + par
                    tp = None if par == 0 else (0, 64)
                    rows = slice(0, 64) if par == 0 else slice(64, 128)
                    smml = em.op(T, lambda t, p=p, h=h, tp=tp, rows=rows:
                                 t.matmul(ps_st[rows, 64 * p:64 * p + 64],
                                          btm[:, h // 2,
                                              64 * (h % 2):64 * (h % 2) + 64],
                                          xdtm[:, h // 2,
                                               64 * (h % 2):64 * (h % 2) + 64],
                                          start=True, stop=True,
                                          tile_position=tp),
                                 deps=[cvb, xd, pc["su"]] if first else [],
                                 inc=(p == 3 and par == 1))
                    first = False
            ymm0 = None
            yoffl = None
            for p in range(4):
                for par in (0, 1):
                    h = 2 * p + par
                    tp = None if par == 0 else (0, 64)
                    tpo = None if par == 0 else (64, 64)
                    rows = slice(0, 64) if par == 0 else slice(64, 128)
                    first = (p == 0 and par == 0)
                    tokmm = em.op(T, lambda t, p=p, h=h, tp=tp, rows=rows:
                                  t.matmul(ps_y[rows, 128 * p:128 * p + 128],
                                           xtm[:, h // 2,
                                               64 * (h % 2):64 * (h % 2) + 64],
                                           prod[:, 128 * h:128 * h + 128],
                                           start=True, stop=False,
                                           tile_position=tp),
                                  deps=[pr1, cvx, cxp, pc["ev"], pc["bd"]]
                                  if first else [], inc=first)
                    if first:
                        ymm0 = tokmm
                    yoffl = em.op(T, lambda t, p=p, h=h, par=par, rows=rows:
                                  t.matmul(ps_y[rows, 128 * p:128 * p + 128],
                                           state_bfz[:, h, :],
                                           cexp[:, p, :],
                                           start=False, stop=True,
                                           tile_position=(None if par == 0
                                                          else (0, 64))),
                                  inc=(p == 3 and par == 1))
            pc["ymm0"] = ymm0
            ev_toks = []
            for p in range(4):
                e1 = em.op(V, lambda v, p=p, rr=rr, cL=cL:
                           v.scalar_tensor_tensor(
                               arg_sb[:, 128 * p:128 * p + 128],
                               xbc_bf[:, rr, p, cL:cL + 128],
                               drep_sb[:, p:p + 1],
                               ps_y[:, 128 * p:128 * p + 128],
                               ALU.mult, ALU.add),
                           deps=[yoffl, g3, tok_c] if p == 0 else [])
                e2 = em.op(V, lambda v, p=p, ci=ci, cL=cL:
                           v.tensor_tensor(ygst[:, ci % 2, p, :],
                                           arg_sb[:, 128 * p:128 * p + 128],
                                           zg_ch[:, p, cL:cL + 128],
                                           ALU.mult),
                           deps=[e1, zg_tok[(tc, p)],
                                 yg_dma[-2] if len(yg_dma) >= 2 else None])
                ev_toks.append(e2)
            pc["e2s"] = ev_toks
            pc["ev"] = ev_toks[-1]
            cg = tc * 4 + ci
            yg_dma.append(em.dma(Y, syg[ci % 2], lambda y, ci=ci, cg=cg:
                                 y.dma_start(
                                     out=rearr(a2am_in)[:, 0:4,
                                                        cg * 128:cg * 128 + 128],
                                     in_=ygst[:, ci % 2, :, :]),
                                 deps=ev_toks))
            sul = None
            for p in range(4):
                sul = em.op(V, lambda v, p=p: v.scalar_tensor_tensor(
                    state_f[:, p, :], state_f[:, p, :], decpair[:, p:p + 1],
                    ps_st[:, 64 * p:64 * p + 64], ALU.mult, ALU.add),
                    deps=[smml, cv4, yoffl] if p == 0 else [])
            pc["su"] = sul
            bdl = None
            for p in range(4):
                be = em.op(V, lambda v, p=p: v.tensor_copy(
                    state_bfz[0:64, 2 * p, :], state_f[0:64, p, :]),
                    deps=[sul] if p == 0 else [])
                bdl = em.op(V, lambda v, p=p: v.tensor_copy(
                    state_bfz[64:128, 2 * p + 1, :], state_f[64:128, p, :]),
                    deps=[be])
            pc["bd"] = bdl
        tc_done[tc] = [em.etok(T), em.etok(V), em.etok(SS)]

    while qsteps:
        qsteps.pop(0)()
    tok_a2am = coll("AllGather", ALU.bypass, G_BATCH, a2am_in, a2am_out,
                    deps=[yg_dma[-1], yg_dma[-2]])
    em.barrier()
    em.flush("p1")
    es1.close()
    es_win.close()

    es2 = contextlib.ExitStack()

    def sb2(name, shape, dt=F32):
        return es2.enter_context(nc.sbuf_tensor(name, list(shape), dt))
    x_sb = root.enter_context(nc.sbuf_tensor("x_sb", [128, DT, Sq], F32))
    yg_own = sb2("yg_own", (128, DT, Sq), BF16)
    yn_bf = sb2("yn_bf", (128, DT, Sq), BF16)
    sqs2 = sb2("sqs2", (128, 2, Sq))
    rr_sb = sb2("rr_sb", (128, Sq))
    wot = sb2("wot", (128, 2, DT, 128), BF16)
    ps_m2 = es2.enter_context(nc.psum_tensor("ps_m2", [128, Sq], F32))
    ps_o = es2.enter_context(nc.psum_tensor("ps_o", [128, 2, W], F32))

    ldyg = em.dma_percore(Y, smisc, lambda y, j: y.dma_start(
        out=yg_own[:, :, :],
        in_=rearr(a2am_out)[:, :, j * Sq:j * Sq + Sq]), deps=[tok_a2am])
    ldu = []
    for dt in range(DT):
        ldu.append(em.dma(Y, su[0], lambda y, dt=dt: y.dma_start(
            out=x_sb[:, dt, :], in_=rearr(u_q)[:, dt, :])))
    ldu_all = ldu[-1]
    mmq = None
    sq2t = [None] * DT
    for dt in range(DT):
        sq2t[dt] = em.op(SS, lambda s, dt=dt: s.activation(
            sqs2[:, dt % 2, :], yg_own[:, dt, :], AF.Square),
            deps=[ldyg] + ([mmq] if dt >= 2 else []))
        for ch in range(nch):
            mmq = em.op(T, lambda t, dt=dt, ch=ch: t.matmul(
                ps_m2[:, W * ch:W * ch + W], onesLN[:, :],
                sqs2[:, dt % 2, W * ch:W * ch + W],
                start=(dt == 0), stop=(dt == DT - 1)),
                deps=[sq2t[dt]] if ch == 0 else [], inc=(ch == nch - 1))
    w1 = em.op(SS, lambda s: s.activation(rr_sb[:, :], ps_m2[:, :], AF.Sqrt,
                                          bias=cst_sb[:, 0:1]), deps=[mmq, c4])
    w2 = em.op(V, lambda v: v.reciprocal(rr_sb[:, :], rr_sb[:, :]), deps=[w1])
    ynt = [None] * DT
    for dt in range(DT):
        ynt[dt] = em.op(V, lambda v, dt=dt: v.scalar_tensor_tensor(
            yn_bf[:, dt, :], yg_own[:, dt, :], rmsw_sb[:, dt:dt + 1],
            rr_sb[:, :], ALU.mult, ALU.mult), deps=[w2, ldyg, tok_c])
    evo = {}
    for oc in range(DT):
        lw = em.dma(G, sg2[oc % 2], lambda g, oc=oc: g.dma_start(
            out=wot[:, oc % 2, :, :],
            in_=rearr(woutT)[:, :, 128 * oc:128 * oc + 128]),
            deps=[evo.get((oc - 2, nch - 1))])
        for ch in range(nch):
            for dt in range(DT):
                mo = em.op(T, lambda t, oc=oc, ch=ch, dt=dt: t.matmul(
                    ps_o[:, (oc * nch + ch) % 2, :], wot[:, oc % 2, dt, :],
                    yn_bf[:, dt, W * ch:W * ch + W],
                    start=(dt == 0), stop=(dt == DT - 1)),
                    deps=([lw, ynt[DT - 1],
                           evo.get(divmod(oc * nch + ch - 2, nch))]
                          if dt == 0 else []),
                    inc=(dt == DT - 1))
            evo[(oc, ch)] = em.op(V, lambda v, oc=oc, ch=ch: v.tensor_tensor(
                x_sb[:, oc, W * ch:W * ch + W],
                x_sb[:, oc, W * ch:W * ch + W],
                ps_o[:, (oc * nch + ch) % 2, :], ALU.add),
                deps=[mo, ldu_all])
    em.barrier()
    em.flush("p1b")
    es2.close()

    es3 = contextlib.ExitStack()

    def sb3(name, shape, dt=F32):
        return es3.enter_context(nc.sbuf_tensor(name, list(shape), dt))
    Hh = Sq // 2
    xn_own = sb3("xn_own", (128, DT, Sq), BF16)
    hid = sb3("hid", (128, 64, Hh), BF16)
    wub = sb3("wub", (128, 2, DT, 256), BF16)
    wdb = sb3("wdb", (128, 2, 32, 128), BF16)
    ps_h = es3.enter_context(nc.psum_tensor("ps_h", [128, 2, 512], F32))
    ps_d = es3.enter_context(nc.psum_tensor("ps_d", [128, 2, 512], F32))

    ldxn = em.dma(Y, smisc, lambda y: y.dma_start(
        out=xn_own[:, :, :], in_=rearr(ag0_in)[:, :, :]))
    relu_tok = {}
    dn_tok = {}
    md_last = None
    for half in range(2):
        ho = Hh * half
        for hb in range(32):
            r_ = hb // 4
            c0_ = 256 * (hb % 4)
            lwu = em.dma(Y, swu[hb % 2], lambda y, hb=hb, r_=r_, c0_=c0_, half=half:
                         y.dma_start(out=wub[:, hb % 2, :, :],
                                     in_=rearr(agsu_out)[:, 16 * r_:16 * r_ + 16,
                                                         c0_:c0_ + 256]),
                         deps=[qres["agsu"], relu_tok.get((half, 2 * hb - 3)),
                               relu_tok.get((half - 1, 63)) if hb < 2 else None,
                               relu_tok.get((half - 1, 62)) if hb < 2 else None])
            for sub in range(2):
                ht = 2 * hb + sub
                for dt in range(DT):
                    mt = em.op(T, lambda t, hb=hb, sub=sub, dt=dt, ho=ho, ht=ht:
                               t.matmul(ps_h[:, ht % 2, 0:Hh],
                                        wub[:, hb % 2, dt,
                                            128 * sub:128 * sub + 128],
                                        xn_own[:, dt, ho:ho + Hh],
                                        start=(dt == 0), stop=(dt == DT - 1)),
                               deps=([lwu, ldxn, relu_tok.get((half, ht - 2)),
                                      md_last] if dt == 0 else []),
                               inc=(dt == DT - 1))
                relu_tok[(half, ht)] = em.op(
                    SS, lambda s, ht=ht: s.activation(
                        hid[:, ht, :], ps_h[:, ht % 2, 0:Hh], AF.Relu,
                        scale=upsc_sb[:, ht:ht + 1]),
                    deps=[mt, tok_c])
        for ocb in range(DT):
            lwd = {}
            for hf in range(2):
                lwd[hf] = em.dma(Y, swd[hf], lambda y, ocb=ocb, hf=hf:
                                 y.dma_start(
                    out=wdb[:, hf, :, :],
                    in_=rearr(agsd_out)[:, 64 * (ocb // 2) + 32 * hf:
                                        64 * (ocb // 2) + 32 * hf + 32,
                                        128 * (ocb % 2):128 * (ocb % 2) + 128]),
                    deps=[qres["agsd"], dn_tok.get((half, ocb - 1)),
                          dn_tok.get((half - 1, DT - 1)) if ocb == 0
                          else None])
            for it in range(64):
                md = em.op(T, lambda t, ocb=ocb, it=it: t.matmul(
                    ps_d[:, ocb % 2, 0:Hh], wdb[:, it // 32, it % 32, :],
                    hid[:, it, :],
                    start=(it == 0), stop=(it == 63)),
                    deps=([lwd[0], lwd[1], relu_tok[(half, 63)],
                           dn_tok.get((half, ocb - 2)),
                           dn_tok.get((half - 1, DT - 2 + ocb)) if ocb < 2
                           else None] if it == 0 else []),
                    inc=(it == 63))
            md_last = md
            dn_tok[(half, ocb)] = em.op(V, lambda v, ocb=ocb, ho=ho:
                                        v.scalar_tensor_tensor(
                                            x_sb[:, ocb, ho:ho + Hh],
                                            ps_d[:, ocb % 2, 0:Hh],
                                            dscmb_sb[:, ocb:ocb + 1],
                                            x_sb[:, ocb, ho:ho + Hh],
                                            ALU.mult, ALU.add),
                                        deps=[md, k3])
    em.barrier()
    em.flush("p2")
    es3.close()

    comb2 = late.enter_context(nc.sbuf_tensor("comb2", [128, Sq], F32))
    dq2r = late.enter_context(nc.sbuf_tensor("dq2r", [128, Sq], F32))
    dq1r = late.enter_context(nc.sbuf_tensor("dq1r", [128, Sq], F32))
    mdn = late.enter_context(nc.sbuf_tensor("mdn", [128, Sq], F32))
    es4 = contextlib.ExitStack()

    def sb4(name, shape, dt=F32):
        return es4.enter_context(nc.sbuf_tensor(name, list(shape), dt))
    xq_scope = contextlib.ExitStack()
    xq = xq_scope.enter_context(nc.sbuf_tensor("xq", [128, DT, Sq], BF16))
    xnt = sb4("xnt", (128, 2, Sq))
    sq4 = sb4("sq4", (128, 2, Sq))
    m4 = sb4("m4", (128, Sq))
    rs4 = sb4("rs4", (128, Sq))
    t131 = sb4("t131", (128, Sq))
    xnst4 = sb4("xnst4", (128, 2, Sq), BF16)
    mrun = sb4("mrun", (128, Sq))
    mx_tm = sb4("mx_tm", (128, 8))
    srow = sb4("srow", (8, 128))
    scl_sb = sb4("scl_sb", (128, Sq))
    comb1 = sb4("comb1", (128, Sq))
    ps4_m = es4.enter_context(nc.psum_tensor("ps4_m", [128, Sq], F32))
    ps4_v = es4.enter_context(nc.psum_tensor("ps4_v", [128, Sq], F32))
    ps4_t = es4.enter_context(nc.psum_tensor("ps4_t", [128, Sq], F32))

    mmv4 = None
    sq4t = [None] * DT
    for dt in range(DT):
        sq4t[dt] = em.op(SS, lambda s, dt=dt: s.activation(
            sq4[:, dt % 2, :], x_sb[:, dt, :], AF.Square),
            deps=[mmv4] if dt >= 2 else [])
        for ch in range(nch):
            em.op(T, lambda t, dt=dt, ch=ch: t.matmul(
                ps4_m[:, W * ch:W * ch + W], onesLN[:, :],
                x_sb[:, dt, W * ch:W * ch + W],
                start=(dt == 0), stop=(dt == DT - 1)), inc=False)
            mmv4 = em.op(T, lambda t, dt=dt, ch=ch: t.matmul(
                ps4_v[:, W * ch:W * ch + W], onesLN[:, :],
                sq4[:, dt % 2, W * ch:W * ch + W],
                start=(dt == 0), stop=(dt == DT - 1)),
                deps=[sq4t[dt]] if ch == 0 else [], inc=(ch == nch - 1))
    u5 = em.op(V, lambda v: v.tensor_copy(m4[:, :], ps4_m[:, :]), deps=[mmv4])
    u1 = em.op(V, lambda v: v.tensor_tensor(t131[:, :], m4[:, :], m4[:, :],
                                            ALU.mult), deps=[u5])
    u2 = em.op(V, lambda v: v.tensor_tensor(t131[:, :], ps4_v[:, :], t131[:, :],
                                            ALU.subtract), deps=[u1])
    u3 = em.op(SS, lambda s: s.activation(t131[:, :], t131[:, :], AF.Sqrt,
                                          bias=cst_sb[:, 0:1]), deps=[u2, c4])
    u4 = em.op(V, lambda v: v.reciprocal(rs4[:, :], t131[:, :]), deps=[u3])
    mm5 = None
    mm5x = {}
    std4 = [None] * DT
    xn_last = None
    for dt in range(DT):
        a1 = em.op(V, lambda v, dt=dt: v.tensor_tensor(
            t131[:, :], x_sb[:, dt, :], m4[:, :], ALU.subtract),
            deps=[u4, u5] + ([xn_last] if dt >= 1 else []))
        a2 = em.op(V, lambda v, dt=dt: v.tensor_tensor(
            xnt[:, dt % 2, :], t131[:, :], rs4[:, :], ALU.mult),
            deps=[a1, std4[dt - 2] if dt >= 2 else None,
                  mm5x[dt - 2] if dt >= 2 else None])
        xn_last = a2
        a3 = em.op(SS, lambda s, dt=dt: s.copy(xnst4[:, dt % 2, :],
                                               xnt[:, dt % 2, :]),
                   deps=[a2] + ([std4[dt - 2]] if dt >= 2 else []))
        std4[dt] = em.dma(Y, sstp[dt % 2], lambda y, dt=dt: y.dma_start(
            out=rearr(ag1_in)[:, dt, :], in_=xnst4[:, dt % 2, :]), deps=[a3])
        s1 = em.op(SS, lambda s, dt=dt: s.activation(
            sq4[:, dt % 2, :], xnt[:, dt % 2, :], AF.Square),
            deps=[a2, a3, mm5x[dt - 2] if dt >= 2 else None])
        for ch in range(nch):
            mm5 = em.op(T, lambda t, dt=dt, ch=ch: t.matmul(
                ps4_v[:, W * ch:W * ch + W], onesLN[:, :],
                sq4[:, dt % 2, W * ch:W * ch + W],
                start=(dt == 0), stop=(dt == DT - 1)),
                deps=[s1, u1, u2] if ch == 0 else [], inc=(ch == nch - 1))
        mm5x[dt] = mm5
    tok_ag1 = coll("AllGather", ALU.bypass, G_BATCH, ag1_in, ag1_out,
                   deps=[std4[DT - 1], std4[DT - 2]])
    w3 = em.op(SS, lambda s: s.activation(t131[:, :], ps4_v[:, :], AF.Sqrt,
                                          bias=cst_sb[:, 0:1]),
               deps=[mm5, u4, c4])
    rr2 = sb4("rr2", (128, Sq))
    w4 = em.op(V, lambda v: v.reciprocal(rr2[:, :], t131[:, :]),
               deps=[w3, xn_last])
    mz = em.op(V, lambda v: v.memset(mrun[:, :], 0.0))
    mtk = mz
    for dt in range(DT):
        b1 = em.op(V, lambda v, dt=dt: v.tensor_tensor(
            t131[:, :], x_sb[:, dt, :], m4[:, :], ALU.subtract),
            deps=[w4, mtk])
        b2 = em.op(V, lambda v, dt=dt: v.tensor_tensor(
            xnt[:, dt % 2, :], t131[:, :], rs4[:, :], ALU.mult), deps=[b1])
        a1 = em.op(V, lambda v, dt=dt: v.scalar_tensor_tensor(
            t131[:, :], xnt[:, dt % 2, :], gnw_sb[:, dt:dt + 1], rr2[:, :],
            ALU.mult, ALU.mult), deps=[b2, tok_c])
        aab = em.op(V, lambda v: v.scalar_tensor_tensor(
            t131[:, :], t131[:, :], -1.0, t131[:, :], ALU.mult, ALU.max),
            deps=[a1])
        mtk = em.op(V, lambda v: v.tensor_tensor(mrun[:, :], mrun[:, :],
                                                 t131[:, :], ALU.max),
                    deps=[aab])
    trq = None
    for g_ in range(ntk):
        trq = em.op(T, lambda t, g_=g_: t.transpose(
            ps4_t[:, 128 * g_:128 * g_ + 128], mrun[:, 128 * g_:128 * g_ + 128],
            eyef_sb[:, :]), deps=[mtk, mm5] if g_ == 0 else [],
            inc=(g_ == ntk - 1))
    rmx = None
    for g_ in range(ntk):
        rmx = em.op(V, lambda v, g_=g_: v.reduce_max(
            mx_tm[:, g_:g_ + 1], ps4_t[:, 128 * g_:128 * g_ + 128], AX.X),
            deps=[trq] if g_ == 0 else [])
    c1_ = em.op(V, lambda v: v.tensor_scalar(mx_tm[:, 0:ntk], mx_tm[:, 0:ntk],
                                             1e-5, None, ALU.max), deps=[rmx])
    trs = em.op(T, lambda t: t.transpose(ps4_t[0:ntk, 0:128], mx_tm[:, 0:ntk],
                                         eyef_sb[:, :]), deps=[c1_])
    cvs = em.op(V, lambda v: v.tensor_copy(srow[0:ntk, :], ps4_t[0:ntk, 0:128]),
                deps=[trs])
    krep = None
    for g_ in range(ntk):
        krep = em.op(T, lambda t, g_=g_: t.matmul(
            ps4_m[:, 128 * g_:128 * g_ + 128],
            sel8_sb[0:ntk, 128 * g_:128 * g_ + 128],
            srow[0:ntk, 0:128], start=True, stop=True),
            deps=[cvs, w4] if g_ == 0 else [], inc=(g_ == ntk - 1))
    mxr = em.op(V, lambda v: v.tensor_copy(scl_sb[:, :], ps4_m[:, :]),
                deps=[krep])
    iv1 = em.op(V, lambda v: v.reciprocal(t131[:, :], scl_sb[:, :]),
                deps=[mxr, mtk])
    cb1 = em.op(V, lambda v: v.scalar_tensor_tensor(
        comb1[:, :], t131[:, :], 127.0, rr2[:, :], ALU.mult, ALU.mult),
        deps=[iv1])
    dq1 = em.op(V, lambda v: v.tensor_scalar(
        dq1r[:, :], scl_sb[:, :], invs_sb[:, 0:1], 0.5 / 127.0, ALU.mult,
        ALU.mult), deps=[mxr, qres["q2"]])
    xqt = [None] * DT
    for dt in range(DT):
        b1 = em.op(V, lambda v, dt=dt: v.tensor_tensor(
            t131[:, :], x_sb[:, dt, :], m4[:, :], ALU.subtract),
            deps=[cb1] + ([xqt[dt - 1]] if dt >= 1 else []))
        b2 = em.op(V, lambda v, dt=dt: v.tensor_tensor(
            xnt[:, dt % 2, :], t131[:, :], rs4[:, :], ALU.mult), deps=[b1])
        a1 = em.op(V, lambda v, dt=dt: v.scalar_tensor_tensor(
            t131[:, :], xnt[:, dt % 2, :], gnw_sb[:, dt:dt + 1], comb1[:, :],
            ALU.mult, ALU.mult), deps=[b2])
        xqt[dt] = em.op(V, lambda v, dt=dt: v.tensor_scalar(
            xq[:, dt, :], t131[:, :], MAGIC, MAGIC, ALU.add, ALU.subtract),
            deps=[a1])
    em.barrier()
    em.flush("p3a")
    es4.close()

    es5 = contextlib.ExitStack()

    def sb5(name, shape, dt=F32):
        return es5.enter_context(nc.sbuf_tensor(name, list(shape), dt))
    qg = sb5("qg", (128, 4, DT, 128), BF16)
    tg1 = sb5("tg1", (128, 2, W))
    tg2 = sb5("tg2", (128, 2, W))
    hst = sb5("hst", (128, 2, W))
    hsq = sb5("hsq", (128, 2, W))
    ps_gg = es5.enter_context(nc.psum_tensor("ps_gg", [128, 2, W], F32))
    ps_gu = es5.enter_context(nc.psum_tensor("ps_gu", [128, 2, W], F32))
    ps_sd = es5.enter_context(nc.psum_tensor("ps_sd", [128, Sq], F32))
    mdn2 = sb5("mdn2", (128, Sq))
    mz2 = em.op(V, lambda v: v.memset(mdn[:, :], 0.0))
    mz2b = em.op(V, lambda v: v.memset(mdn2[:, :], 0.0))
    mdn_tok2 = mz2b
    hdma = {}
    mdn_tok = mz2
    ldq = {}
    ev4 = {}
    for ti in range(44):
        for which, cc0 in ((0, 128 * ti), (1, 5632 + 128 * ti)):
            r_ = cc0 // 1408
            c_ = cc0 % 1408
            ldq[(ti, which)] = em.dma(Y, sqg[2 * (ti % 2) + which], lambda y, which=which, r_=r_,
                                      c_=c_, ti=ti: y.dma_start(
                out=qg[:, 2 * (ti % 2) + which, :, :],
                in_=rearr(agqg_out)[:, 16 * r_:16 * r_ + 16, c_:c_ + 128]),
                deps=[qres["agqg"], ev4.get((ti - 2, nch - 1, "e4"))])
        for ch in range(nch):
            co = W * ch
            for dt in range(DT):
                mg = em.op(T, lambda t, ti=ti, ch=ch, dt=dt, co=co: t.matmul(
                    ps_gg[:, ch % 2, :], qg[:, 2 * (ti % 2), dt, :],
                    xq[:, dt, co:co + W],
                    start=(dt == 0), stop=(dt == DT - 1)),
                    deps=([ldq[(ti, 0)], ev4.get((ti - 1, ch, "e1"))]
                          if dt == 0 else []), inc=(dt == DT - 1))
            for dt in range(DT):
                mu = em.op(T, lambda t, ti=ti, ch=ch, dt=dt, co=co: t.matmul(
                    ps_gu[:, ch % 2, :], qg[:, 2 * (ti % 2) + 1, dt, :],
                    xq[:, dt, co:co + W],
                    start=(dt == 0), stop=(dt == DT - 1)),
                    deps=([ldq[(ti, 1)], ev4.get((ti - 1, ch, "e3"))]
                          if dt == 0 else []), inc=(dt == DT - 1))
            e1 = em.op(V, lambda v, ch=ch, co=co: v.tensor_tensor(
                tg1[:, ch % 2, :], ps_gg[:, ch % 2, :], dq1r[:, co:co + W],
                ALU.mult), deps=[mg, dq1, ev4.get((ti - 1, ch, "e3")),
                                 ev4.get((ti - 1, ch, "mdn"))])
            ev4[(ti, ch, "e1")] = e1
            e2 = em.op(SS, lambda s, ch=ch: s.activation(
                tg2[:, ch % 2, :], tg1[:, ch % 2, :], AF.Silu),
                deps=[e1] + ([ev4.get((ti - 1, ch, "e3"))] if ti else []))
            e3 = em.op(V, lambda v, ch=ch: v.tensor_tensor(
                tg1[:, ch % 2, :], tg2[:, ch % 2, :], ps_gu[:, ch % 2, :],
                ALU.mult), deps=[e2, mu])
            ev4[(ti, ch, "e3")] = e3
            e4 = em.op(V, lambda v, ch=ch, co=co: v.tensor_tensor(
                hst[:, ch % 2, :], tg1[:, ch % 2, :], dq1r[:, co:co + W],
                ALU.mult), deps=[e3, hdma.get((ti - 1, ch)),
                                 ev4.get((ti - 1, ch, "e7"))])
            ev4[(ti, ch, "e4")] = e4
            e5 = em.op(V, lambda v, ch=ch, ti=ti: v.tensor_scalar(
                tg1[:, ch % 2, :], hst[:, ch % 2, :], dnw1_sb[:, ti:ti + 1],
                None, ALU.mult), deps=[e4, tok_c])
            e5b = em.op(V, lambda v, ch=ch: v.scalar_tensor_tensor(
                tg1[:, ch % 2, :], tg1[:, ch % 2, :], -1.0, tg1[:, ch % 2, :],
                ALU.mult, ALU.max), deps=[e5])
            if ti % 2 == 0:
                mdn_tok = em.op(V, lambda v, ch=ch, co=co: v.tensor_tensor(
                    mdn[:, co:co + W], mdn[:, co:co + W], tg1[:, ch % 2, :],
                    ALU.max), deps=[e5b, mdn_tok])
                ev4[(ti, ch, "mdn")] = mdn_tok
            else:
                mdn_tok2 = em.op(V, lambda v, ch=ch, co=co: v.tensor_tensor(
                    mdn2[:, co:co + W], mdn2[:, co:co + W], tg1[:, ch % 2, :],
                    ALU.max), deps=[e5b, mdn_tok2])
                ev4[(ti, ch, "mdn")] = mdn_tok2
            e7 = em.op(SS, lambda s, ch=ch: s.activation(
                hsq[:, ch % 2, :], hst[:, ch % 2, :], AF.Square), deps=[e4])
            ev4[(ti, ch, "e7")] = e7
            em.op(T, lambda t, ti=ti, ch=ch, co=co: t.matmul(
                ps_sd[:, co:co + W], ones56[:, :], hsq[:, ch % 2, :],
                start=(ti == 0), stop=(ti == 43)),
                deps=[e7, c2], inc=(ti == 43 and ch == nch - 1))
            hdma[(ti, ch)] = em.dma(Y, sh2[ch % 2], lambda y, ti=ti, ch=ch, co=co:
                                    y.dma_start(
                                        out=rearr(h_sp)[:, ti, co:co + W],
                                        in_=hst[:, ch % 2, :]),
                                    deps=[e4, e7])
    sd_done = em.etok(T)
    rrdn = sb5("rrdn", (128, Sq))
    w5 = em.op(SS, lambda s: s.activation(rrdn[:, :], ps_sd[:, :], AF.Sqrt,
                                          bias=cst_sb[:, 0:1]), deps=[sd_done, c4])
    w6 = em.op(V, lambda v: v.reciprocal(rrdn[:, :], rrdn[:, :]), deps=[w5])
    mx_tm2 = sb5("mx_tm2", (128, 8))
    mmerge = em.op(V, lambda v: v.tensor_tensor(
        mdn[:, :], mdn[:, :], mdn2[:, :], ALU.max),
        deps=[mdn_tok, mdn_tok2])
    trtoks = []
    rd2 = None
    for g_ in range(ntk):
        tt_ = em.op(T, lambda t, g_=g_: t.transpose(
            ps_gg[:, 0, 0:128], mdn[:, 128 * g_:128 * g_ + 128], eyef_sb[:, :]),
            deps=([mmerge, ev4.get((43, nch - 1, "e1"))] if g_ == 0 else
                  [trtoks[-1][1]]))
        rd2 = em.op(V, lambda v, g_=g_: v.reduce_max(
            mx_tm2[:, g_:g_ + 1], ps_gg[:, 0, 0:128], AX.X), deps=[tt_])
        trtoks.append((tt_, rd2))
    c2_ = em.op(V, lambda v: v.tensor_scalar(mx_tm2[:, 0:ntk], mx_tm2[:, 0:ntk],
                                             1e-5, None, ALU.max), deps=[rd2])
    srow2 = sb5("srow2", (8, 128))
    trs2 = em.op(T, lambda t: t.transpose(ps_gg[0:ntk, 0, 0:128],
                                          mx_tm2[:, 0:ntk], eyef_sb[:, :]),
                 deps=[c2_, rd2])
    cvs2 = em.op(V, lambda v: v.tensor_copy(srow2[0:ntk, :],
                                            ps_gg[0:ntk, 0, 0:128]),
                 deps=[trs2])
    krep2 = None
    for g_ in range(ntk):
        krep2 = em.op(T, lambda t, g_=g_: t.matmul(
            ps_sd[:, 128 * g_:128 * g_ + 128],
            sel8_sb[0:ntk, 128 * g_:128 * g_ + 128],
            srow2[0:ntk, 0:128], start=True, stop=True),
            deps=[cvs2, w6] if g_ == 0 else [], inc=(g_ == ntk - 1))
    mxr2 = em.op(V, lambda v: v.tensor_copy(comb2[:, :], ps_sd[:, :]),
                 deps=[krep2])
    dq2 = em.op(V, lambda v: v.tensor_scalar(
        dq2r[:, :], comb2[:, :], invs_sb[:, 1:2], 0.5 / 127.0, ALU.mult,
        ALU.mult), deps=[mxr2])
    iv2 = em.op(V, lambda v: v.reciprocal(comb2[:, :], comb2[:, :]),
                deps=[dq2])
    cb2 = em.op(V, lambda v: v.scalar_tensor_tensor(
        comb2[:, :], comb2[:, :], 127.0, rrdn[:, :], ALU.mult, ALU.mult),
        deps=[iv2, w6])
    em.barrier()
    em.flush("p4")
    es5.close()
    xq_scope.close()

    attn_scope = contextlib.ExitStack()
    q_bf = attn_scope.enter_context(nc.sbuf_tensor("q_bf", [128, 4, S], BF16))
    k_bf = attn_scope.enter_context(nc.sbuf_tensor("k_bf", [128, S], BF16))
    v_tm = attn_scope.enter_context(
        nc.sbuf_tensor("v_tm", [128, S // 128, 128], BF16))
    es4b = contextlib.ExitStack()

    def sb4b(name, shape, dt=F32):
        return es4b.enter_context(nc.sbuf_tensor(name, list(shape), dt))
    wqv = sb4b("wqv", (128, DT, 768), BF16)
    rhs4 = sb4b("rhs4", (128, 2, DT, 512), BF16)
    ps_qk = es4b.enter_context(nc.psum_tensor("ps_qk", [128, 2, 512], F32))
    ps_vt = es4b.enter_context(nc.psum_tensor("ps_vt", [128, 2, 512], F32))
    em.dma(G, sgm, lambda g: g.dma_start(
        out=wqv[:, :, 0:512], in_=rearr(wq_sl)[:, :, :]))
    em.dma(G, sgm, lambda g: g.dma_start(
        out=wqv[:, :, 512:640], in_=rearr(wk_sl)[:, :, :]))
    lqkv = em.dma(G, sgm, lambda g: g.dma_start(
        out=wqv[:, :, 640:768], in_=rearr(wv_sl)[:, :, :]))
    lq = lk = lv = lqkv
    qev = {}
    for tcg in range(NTC):
        r_ = (512 * tcg) // Sq
        c0_ = (512 * tcg) % Sq
        lr = em.dma(Y, srh[tcg % 2], lambda y, tcg=tcg, r_=r_, c0_=c0_: y.dma_start(
            out=rhs4[:, tcg % 2, :, :],
            in_=rearr(ag1_out)[:, 16 * r_:16 * r_ + 16, c0_:c0_ + 512]),
            deps=[tok_ag1, qev.get((tcg - 2, 4)), qev.get(("v", 4 * tcg - 5))])
        for oc in range(5):
            for dt in range(DT):
                mq = em.op(T, lambda t, tcg=tcg, oc=oc, dt=dt: t.matmul(
                    ps_qk[:, oc % 2, :], wqv[:, dt, 128 * oc:128 * oc + 128],
                    rhs4[:, tcg % 2, dt, :],
                    start=(dt == 0), stop=(dt == DT - 1)),
                    deps=([lr, lq, lk, lv, qev.get((tcg, oc - 2)),
                           qev.get((tcg - 1, oc + 3))] if dt == 0 else []),
                    inc=(dt == DT - 1))
            if oc < 4:
                ev = em.op(SS, lambda s, tcg=tcg, oc=oc: s.copy(
                    q_bf[:, oc, 512 * tcg:512 * tcg + 512], ps_qk[:, oc % 2, :]),
                    deps=[mq])
            else:
                ev = em.op(SS, lambda s, tcg=tcg: s.copy(
                    k_bf[:, 512 * tcg:512 * tcg + 512], ps_qk[:, 0, :]),
                    deps=[mq])
            qev[(tcg, oc)] = ev
        for gg in range(4):
            gt = 4 * tcg + gg
            for dt in range(DT):
                mv = em.op(T, lambda t, tcg=tcg, gg=gg, dt=dt: t.matmul(
                    ps_vt[:, gg % 2, 0:128],
                    rhs4[:, tcg % 2, dt, 128 * gg:128 * gg + 128],
                    wqv[:, dt, 640:768],
                    start=(dt == 0), stop=(dt == DT - 1)),
                    deps=([lv, qev.get((tcg, 4)), qev.get(("v", gt - 2))]
                          if dt == 0 else []),
                    inc=(dt == DT - 1))
            qev[("v", gt)] = em.op(SS, lambda s, gt=gt, gg=gg: s.copy(
                v_tm[:, gt, :], ps_vt[:, gg % 2, 0:128]), deps=[mv])
    em.barrier()
    em.flush("p3b")
    es4b.close()

    es6 = contextlib.ExitStack()

    def sb6(name, shape, dt=F32):
        return es6.enter_context(nc.sbuf_tensor(name, list(shape), dt))
    amask_sb = sb6("amasks", (128, 4, 512))
    am_tok = em.dma(Y, smisc, lambda y: y.dma_start(
        out=amask_sb[:, :, :], in_=amask[:, :, :]))
    pexp5 = sb6("pexp5", (128, 2, 1024), BF16)
    marg = sb6("marg", (128, 2, 1024))
    rcp5 = sb6("rcp5", (128, 2, 512))
    att = sb6("att", (128, 4, S), BF16)
    ps_s5 = es6.enter_context(nc.psum_tensor("ps_s5", [128, 2, 1024], F32))
    ps_o5 = es6.enter_context(nc.psum_tensor("ps_o5", [128, 2, 512], F32))
    ps_n5 = es6.enter_context(nc.psum_tensor("ps_n5", [128, 2, 512], F32))

    SCL = float(1.0 / np.sqrt(128.0))
    att_ev = {}
    exp_slot = {0: None, 1: None}
    kbtok = {0: None, 1: None}
    prev_grp = None
    last_ex = None
    for qc in range(S // 512):
        for pg in range(2):
            nkb = 4 * qc + 4
            av_last = None
            dn_last = None
            for kb in range(nkb):
                diag = kb - 4 * qc
                sc = None
                for gi in range(2):
                    g_ = 2 * pg + gi
                    sc = em.op(T, lambda t, kb=kb, g_=g_, qc=qc, gi=gi:
                               t.matmul(ps_s5[:, kb % 2, 512 * gi:512 * gi + 512],
                                        k_bf[:, 128 * kb:128 * kb + 128],
                                        q_bf[:, g_, 512 * qc:512 * qc + 512],
                                        start=True, stop=True),
                               deps=[exp_slot[kb % 2]] if gi == 0 else [],
                               inc=(gi == 1))
                if diag >= 0:
                    mk = em.op(V, lambda v, kb=kb, diag=diag:
                               v.scalar_tensor_tensor(
                                   marg[:, kb % 2, :], ps_s5[:, kb % 2, :], SCL,
                                   amask_sb[:, diag, None, :].broadcast_to(
                                       [128, 2, 512]),
                                   ALU.mult, ALU.add),
                               deps=[sc, exp_slot[kb % 2], am_tok])
                    ex = em.op(SS, lambda s, kb=kb: s.activation(
                        pexp5[:, kb % 2, :], marg[:, kb % 2, :], AF.Exp,
                        bias=cst_sb[:, 1:2]),
                        deps=[mk, kbtok[kb % 2]])
                else:
                    ex = em.op(SS, lambda s, kb=kb: s.activation(
                        pexp5[:, kb % 2, :], ps_s5[:, kb % 2, :], AF.Exp,
                        scale=SCL, bias=cst_sb[:, 1:2]), deps=[sc, kbtok[kb % 2]])
                exp_slot[kb % 2] = ex
                last_ex = ex
                for gi in range(2):
                    av_last = em.op(T, lambda t, kb=kb, gi=gi, nkb=nkb: t.matmul(
                        ps_o5[:, gi, :], v_tm[:, kb, :],
                        pexp5[:, kb % 2, 512 * gi:512 * gi + 512],
                        start=(kb == 0), stop=(kb == nkb - 1)),
                        deps=([ex, prev_grp] if kb == 0 else [ex]), inc=False)
                    dn_last = em.op(T, lambda t, kb=kb, gi=gi, nkb=nkb: t.matmul(
                        ps_n5[:, gi, :], ones1b[:, :],
                        pexp5[:, kb % 2, 512 * gi:512 * gi + 512],
                        start=(kb == 0), stop=(kb == nkb - 1)),
                        deps=[c3] if (kb == 0 and gi == 0 and qc == 0 and
                                      pg == 0) else [],
                        inc=(gi == 1))
                kbtok[kb % 2] = dn_last
            for gi in range(2):
                g_ = 2 * pg + gi
                rc = em.op(V, lambda v, gi=gi: v.reciprocal(
                    rcp5[:, gi, :], ps_n5[:, gi, :]), deps=[dn_last])
                att_ev[(qc, pg, gi)] = em.op(
                    V, lambda v, g_=g_, gi=gi, qc=qc: v.tensor_tensor(
                        att[:, g_, 512 * qc:512 * qc + 512],
                        ps_o5[:, gi, :], rcp5[:, gi, :], ALU.mult), deps=[rc])
            prev_grp = att_ev[(qc, pg, 1)]
    vfin = em.etok(V)
    adma = []
    for g_ in range(4):
        adma.append(em.dma(Y, sstp[0], lambda y, g_=g_: y.dma_start(
            out=rearr(a2aa_in)[:, g_, :],
            in_=att[:, g_, :]), deps=[vfin]))
    tok_a2aa = coll("AllGather", ALU.bypass, G_BATCH, a2aa_in, a2aa_out,
                    deps=[adma[-1]])
    toks_b2 = [(em.esem[e], em.ecnt[e]) for e in ENGS]
    toks_b2 += [(s.h, s.cnt) for s in em.slots]
    for e_ in ENGS:
        for t_ in toks_b2:
            em.wait(e_, t_)
    em.flush("p5a")
    es6.close()
    attn_scope.close()

    es6b = contextlib.ExitStack()

    def sb6b(name, shape, dt=F32):
        return es6b.enter_context(nc.sbuf_tensor(name, list(shape), dt))
    attn_own = sb6b("attn_own", (128, DT, Sq), BF16)
    wob = sb6b("wob", (128, 2, DT, 128), BF16)
    ps_wo = es6b.enter_context(nc.psum_tensor("ps_wo", [128, 2, W], F32))
    spin2 = None
    for i_ in range(400):
        spin2 = em.op(T, lambda t: t.matmul(
            ps_wo[:, 0, 0:128], eyef_sb[:, :], eyef_sb[:, :],
            start=True, stop=True), inc=(i_ == 399))
    lao = em.dma_percore(Y, smisc, lambda y, j: y.dma_start(
        out=attn_own[:, :, :],
        in_=rearr(a2aa_out)[:, :, j * Sq:j * Sq + Sq]), deps=[tok_a2aa])
    woev = {}
    for oc in range(DT):
        lw = em.dma(G, sg2[oc % 2], lambda g, oc=oc: g.dma_start(
            out=wob[:, oc % 2, :, :],
            in_=rearr(wo)[:, :, 128 * oc:128 * oc + 128]),
            deps=[woev.get((oc - 2, nch - 1))])
        for ch in range(nch):
            for dt in range(DT):
                mwo = em.op(T, lambda t, oc=oc, ch=ch, dt=dt: t.matmul(
                    ps_wo[:, (oc * nch + ch) % 2, :], wob[:, oc % 2, dt, :],
                    attn_own[:, dt, W * ch:W * ch + W],
                    start=(dt == 0), stop=(dt == DT - 1)),
                    deps=([lw, lao, woev.get((oc, ch - 2)),
                           woev.get((oc - 1, ch)),
                           spin2 if (oc == 0 and ch == 0) else None,
                           ] if dt == 0 else []),
                    inc=(dt == DT - 1))
            woev[(oc, ch)] = em.op(V, lambda v, oc=oc, ch=ch: v.tensor_tensor(
                x_sb[:, oc, W * ch:W * ch + W],
                x_sb[:, oc, W * ch:W * ch + W],
                ps_wo[:, (oc * nch + ch) % 2, :], ALU.add),
                deps=[mwo])
    em.barrier()
    em.flush("p5b")
    es6b.close()

    es7 = contextlib.ExitStack()

    def sb7(name, shape, dt=F32):
        return es7.enter_context(nc.sbuf_tensor(name, list(shape), dt))
    xq2 = sb7("xq2", (128, 44, Sq), BF16)
    hld = sb7("hld", (128, 2, Sq))
    t7 = sb7("t7", (128, Sq))
    q2b = sb7("q2b", (128, 2, 22, 128), BF16)
    ps7_d = es7.enter_context(nc.psum_tensor("ps7_d", [128, 2, W], F32))

    x2t = {}
    for it in range(44):
        lh = em.dma(Y, shl[it % 2], lambda y, it=it: y.dma_start(
            out=hld[:, it % 2, :], in_=rearr(h_sp)[:, it, :]),
            deps=[x2t.get(it - 2)])
        a1 = em.op(V, lambda v, it=it: v.scalar_tensor_tensor(
            t7[:, :], hld[:, it % 2, :], dnw1_sb[:, it:it + 1], comb2[:, :],
            ALU.mult, ALU.mult),
            deps=[lh, cb2] + ([x2t[it - 1]] if it >= 1 else []))
        x2t[it] = em.op(V, lambda v, it=it: v.tensor_scalar(
            xq2[:, it, :], t7[:, :], MAGIC, MAGIC, ALU.add, ALU.subtract),
            deps=[a1])

    dnev = {}
    for ocb in range(DT):
        lq2 = {}
        for hf in range(2):
            lq2[hf] = em.dma(Y, sq2s[hf], lambda y, ocb=ocb, hf=hf:
                             y.dma_start(
                out=q2b[:, hf, :, :],
                in_=rearr(agqd_out)[:, 44 * (ocb // 2) + 22 * hf:
                                    44 * (ocb // 2) + 22 * hf + 22,
                                    128 * (ocb % 2):128 * (ocb % 2) + 128]),
                deps=[qres["agqd"], dnev.get((ocb - 1, nch - 1))])
        for ch in range(nch):
            for it in range(44):
                md = em.op(T, lambda t, ocb=ocb, ch=ch, it=it: t.matmul(
                    ps7_d[:, ch % 2, :], q2b[:, it // 22, it % 22, :],
                    xq2[:, it, W * ch:W * ch + W],
                    start=(it == 0), stop=(it == 43)),
                    deps=([lq2[0], lq2[1], x2t[43], dnev.get((ocb - 1, ch))]
                          if it == 0 else []),
                    inc=(it == 43))
            e1 = em.op(V, lambda v, ch=ch: v.tensor_tensor(
                t7[:, 0:W], ps7_d[:, ch % 2, :],
                dq2r[:, W * ch:W * ch + W], ALU.mult),
                deps=[md, dq2])
            dnev[(ocb, ch)] = em.op(V, lambda v, ocb=ocb, ch=ch: v.tensor_tensor(
                x_sb[:, ocb, W * ch:W * ch + W],
                x_sb[:, ocb, W * ch:W * ch + W], t7[:, 0:W], ALU.add),
                deps=[e1])
    em.barrier()
    em.flush("p6")
    es7.close()

    fstore = []
    for dt in range(DT):
        fstore.append(em.dma(Y, sstp[0], lambda y, dt=dt: y.dma_start(
            out=rearr(out_q)[:, dt, :], in_=x_sb[:, dt, :])))
    em.wait(Y, fstore[-1])
    em.barrier()
    em.flush("pfin")
    late.close()
    root.close()
    return nc



def _prep_inputs(inputs, S):
    u = np.asarray(inputs["L0_attn_u"], np.float32)
    W_in = np.asarray(inputs["L0_attn_W_in"], np.float32)
    z_bias = np.asarray(inputs["L0_attn_z_bias"], np.float32)
    conv_w = np.asarray(inputs["L0_attn_conv_w"], np.float32)
    conv_b = np.asarray(inputs["L0_attn_conv_b"], np.float32)
    Dv = np.asarray(inputs["L0_attn_D"], np.float32)
    rms_w = np.asarray(inputs["L0_attn_rms_w"], np.float32)
    W_out = np.asarray(inputs["L0_attn_W_out"], np.float32)
    up_w = np.asarray(inputs["L0_ffn_up_weight"], np.float32)
    dn_w = np.asarray(inputs["L0_ffn_down_weight"], np.float32)
    up_sc = np.asarray(inputs["L0_ffn_up_scale"], np.float32)
    dn_sc = np.asarray(inputs["L0_ffn_down_scale"], np.float32)
    out_sc = np.asarray(inputs["L0_ffn_output_scale"], np.float32)
    Wq = np.asarray(inputs["L1_attn_Wq"], np.float32)
    Wk = np.asarray(inputs["L1_attn_Wk"], np.float32)
    Wv = np.asarray(inputs["L1_attn_Wv"], np.float32)
    Wo = np.asarray(inputs["L1_attn_Wo"], np.float32)
    gate_w = np.asarray(inputs["L1_ffn_gate_w"], np.float32)
    down_w = np.asarray(inputs["L1_ffn_down_w"], np.float32)
    gate_nw = np.asarray(inputs["L1_ffn_gate_norm_w"], np.float32)
    down_nw = np.asarray(inputs["L1_ffn_down_norm_w"], np.float32)

    Sq = S // 4
    C = np.ascontiguousarray

    def fm(vec, ntile):
        return C(vec.reshape(ntile, 128).T)

    winT_full = C(W_in.T)
    woutT = C(W_out.T)
    upwT = C(up_w.T)
    dnwT = C(dn_w.T)
    gwT = C(gate_w.T)
    dw1T = C(down_w.T)

    eye_b = np.eye(128, dtype=bf16)
    eye_f = np.eye(128, dtype=np.float32)
    s_idx = np.arange(128)
    maskTm = np.where(s_idx[:, None] <= np.arange(128)[None, :], 0.0,
                      NEG).astype(np.float32)
    sel8m = np.zeros((8, 1024), np.float32)
    for h_ in range(8):
        sel8m[h_, 128 * h_:128 * h_ + 128] = 1.0
    am = np.empty((128, 4, 512), np.float32)
    for d_ in range(4):
        am[:, d_, :] = np.where(
            128 * d_ + s_idx[:, None] <= np.arange(512)[None, :], 0.0, NEG)

    in_maps = []
    for core in range(8):
        b, j = core // 4, core % 4
        h0 = 8 * j
        xsl = slice(64 * h0, 64 * h0 + 512)
        idx = np.concatenate([np.arange(64 * h0, 64 * h0 + 512),
                              np.arange(2048 + 64 * h0, 2048 + 64 * h0 + 512),
                              np.arange(4096 + 64 * h0, 4096 + 64 * h0 + 512),
                              np.arange(6144 + 64 * h0, 6144 + 64 * h0 + 512),
                              np.arange(8192 + h0, 8192 + h0 + 8)])
        cidx = np.concatenate([np.arange(64 * h0, 64 * h0 + 512),
                               np.arange(2048 + 64 * h0, 2048 + 64 * h0 + 512),
                               np.arange(4096 + 64 * h0, 4096 + 64 * h0 + 512)])
        m = dict(
            u_q=C(u[b, Sq * j:Sq * (j + 1), :].T),
            winT=C(winT_full[:, idx]),
            convw=C(conv_w[cidx].reshape(12, 128, 4).transpose(1, 0, 2)),
            convb=fm(conv_b[cidx], 12),
            zb=fm(z_bias[xsl], 4),
            drep=fm(np.repeat(Dv[h0:h0 + 8], 64), 4),
            rmsw=fm(rms_w, DT),
            woutT=woutT,
            upw_sl=C(upwT[:, 1024 * core:1024 * core + 1024]),
            dnw_sl=C(dnwT[:, 256 * core:256 * core + 256]),
            upsc=fm(up_sc, 64),
            dnsc=fm(dn_sc, DT),
            outsc=out_sc.reshape(1, 1),
            wq_sl=C(Wq[:, 512 * j:512 * j + 512]),
            wk_sl=C(Wk[:, 128 * j:128 * j + 128]),
            wv_sl=C(Wv[:, 128 * j:128 * j + 128]),
            wo=Wo,
            gw_sl=C(gwT[:, 1408 * core:1408 * core + 1408]),
            dw1_sl=C(dw1T[:, 256 * core:256 * core + 256]),
            gnw=fm(gate_nw, DT),
            dnw1=fm(down_nw, 44),
            maskT=maskTm,
            amask=am,
            eye_bf=eye_b,
            eye_f=eye_f,
            sel8=sel8m,
        )
        in_maps.append(m)
    return in_maps


def kernel(**inputs):
    u = np.asarray(inputs["L0_attn_u"])
    S = u.shape[1]
    in_maps = _prep_inputs(inputs, S)
    nc = build(S)
    res = run_bass_kernel_spmd(nc, in_maps, core_ids=list(range(8)))
    Sq = S // 4
    out = np.empty((2, S, D), np.float32)
    for core in range(8):
        b, j = core // 4, core % 4
        out[b, Sq * j:Sq * (j + 1), :] = np.asarray(
            res.results[core]["out_q"]).T
    return out
